# version 39
# baseline (speedup 1.0000x reference)
import contextlib

import numpy as np
import ml_dtypes

import concourse.bass as bass
import concourse.mybir as mybir
from concourse.bass_utils import run_bass_kernel_spmd

F32 = mybir.dt.float32
BF16 = mybir.dt.bfloat16
AF = mybir.ActivationFunctionType
ALU = mybir.AluOpType
AX = mybir.AxisListType

TOK = 128
NT = 17
DM = 1024
H = 8
DH = 128
CH = 8
WIN = 16
EPS_LN = 1e-5
EPS_RMS = 1e-6
NEG = -1e30

ENGINES = ("pe", "act", "dve", "pool", "sp")


class Buf:
    __slots__ = ("name", "writer", "readers")

    def __init__(self, name):
        self.name = name
        self.writer = None
        self.readers = []


_DEF_COST = {"pe": 250.0, "act": 1150.0, "dve": 900.0, "pool": 2350.0,
             "sp": 300.0}
_DMA_LAT = 2500.0
_SEM_LAT = 300.0


class Node:
    __slots__ = ("nid", "eng", "fns", "dma_chan", "deps", "cost", "bias",
                 "prio", "start", "finish", "seq")

    def __init__(self, nid, eng, fns, dma_chan, cost, bias):
        self.nid = nid
        self.eng = eng
        self.fns = fns
        self.dma_chan = dma_chan
        self.deps = set()
        self.cost = cost
        self.bias = bias
        self.prio = 0.0
        self.start = 0.0
        self.finish = 0.0
        self.seq = None


class Sched:

    def __init__(self):
        self.nodes = []

    def group(self, eng, fns, reads=(), writes=(), dma_chan=None, cost=None,
              bias=0.0):
        if cost is None:
            cost = _DEF_COST[eng] * len(fns)
        n = Node(len(self.nodes), eng, fns, dma_chan, cost, bias)
        self.nodes.append(n)
        for b in reads:
            if b.writer is not None:
                n.deps.add(b.writer)
        for b in writes:
            if b.writer is not None:
                n.deps.add(b.writer)
            for r in b.readers:
                n.deps.add(r)
        n.deps.discard(n.nid)
        for b in writes:
            b.writer = n.nid
            b.readers = []
        for b in reads:
            b.readers.append(n.nid)
        return n.nid

    def finalize(self):
        nodes = self.nodes
        succs = [[] for _ in nodes]
        npred = [len(n.deps) for n in nodes]
        for n in nodes:
            for d in n.deps:
                succs[d].append(n.nid)
        indeg_out = [len(succs[i]) for i in range(len(nodes))]
        stack = [i for i in range(len(nodes)) if indeg_out[i] == 0]
        topo_r = []
        while stack:
            i = stack.pop()
            topo_r.append(i)
            for d in nodes[i].deps:
                indeg_out[d] -= 1
                if indeg_out[d] == 0:
                    stack.append(d)
        for i in topo_r:
            m = 0.0
            for s in succs[i]:
                if nodes[s].prio > m:
                    m = nodes[s].prio
            nodes[i].prio = nodes[i].cost + nodes[i].bias + m
        WINDOW = 800.0
        eng_time = {e: 0.0 for e in ENGINES}
        ready_time = [0.0] * len(nodes)
        ready = [n.nid for n in nodes if npred[n.nid] == 0]
        sched_order = {e: [] for e in ENGINES}
        scheduled = 0
        while ready:
            s_min = None
            for nid in ready:
                n = nodes[nid]
                s = max(ready_time[nid], eng_time[n.eng])
                if s_min is None or s < s_min:
                    s_min = s
            best = None
            bestp = None
            for nid in ready:
                n = nodes[nid]
                s = max(ready_time[nid], eng_time[n.eng])
                if s <= s_min + WINDOW and (bestp is None or n.prio > bestp):
                    bestp = n.prio
                    best = nid
            nid = best
            ready.remove(nid)
            n = nodes[nid]
            start = max(ready_time[nid], eng_time[n.eng])
            n.start = start
            lat = _DMA_LAT if n.dma_chan is not None else 0.0
            n.finish = start + n.cost + lat
            eng_time[n.eng] = start + n.cost
            sched_order[n.eng].append(nid)
            scheduled += 1
            for s2 in succs[nid]:
                npred[s2] -= 1
                f = n.finish + (0.0 if nodes[s2].eng == n.eng else _SEM_LAT)
                if ready_time[s2] < f:
                    ready_time[s2] = f
                if npred[s2] == 0:
                    ready.append(s2)
        assert scheduled == len(nodes), (scheduled, len(nodes))
        dma_count = {}
        for e in ENGINES:
            seq = 0
            for nid in sched_order[e]:
                n = nodes[nid]
                if n.dma_chan is not None:
                    c = dma_count.setdefault(n.dma_chan, 0)
                    n.seq = ("dma_" + n.dma_chan, c + 16 * len(n.fns))
                    dma_count[n.dma_chan] = c + 16 * len(n.fns)
                else:
                    seq += 1
                    n.seq = (e, seq)
        items = {e: [] for e in ENGINES}
        hwm = {e: {} for e in ENGINES}
        for e in ENGINES:
            for nid in sched_order[e]:
                n = nodes[nid]
                toks = {}
                for d in n.deps:
                    s, v = nodes[d].seq
                    if toks.get(s, 0) < v:
                        toks[s] = v
                for s in sorted(toks):
                    v = toks[s]
                    if hwm[e].get(s, 0) >= v:
                        continue
                    hwm[e][s] = v
                    items[e].append(("wait", s, v))
                if n.dma_chan is not None:
                    for fn in n.fns:
                        items[e].append(("dma", fn, n.dma_chan))
                else:
                    nf = len(n.fns)
                    for i, fn in enumerate(n.fns):
                        items[e].append(("inst", fn, i == nf - 1))
        return items, dma_count


def build_program():
    nc = bass.Bass("TRN2", target_bir_lowering=False)

    x_hbm = nc.dram_tensor("x", [NT * TOK, DM], F32, kind="ExternalInput").ap()
    w_hbm = {}
    for wname in ("wq", "wk", "wv", "wo", "wr", "wkf", "wvf"):
        w_hbm[wname] = nc.dram_tensor(wname, [DM, DM], F32, kind="ExternalInput").ap()
    vecs_hbm = nc.dram_tensor("vecs", [32, 128], F32, kind="ExternalInput").ap()
    biasc_hbm = nc.dram_tensor("biasc", [128, H * 128], BF16, kind="ExternalInput").ap()
    biash_w_hbm = nc.dram_tensor("biash_w", [128, H * 16], BF16, kind="ExternalInput").ap()
    biash_0_hbm = nc.dram_tensor("biash_0", [128, H * 16], BF16, kind="ExternalInput").ap()
    biash_n_hbm = nc.dram_tensor("biash_n", [128, H * 16], BF16, kind="ExternalInput").ap()
    idb_hbm = nc.dram_tensor("idb", [128, 128], BF16, kind="ExternalInput").ap()
    idf_hbm = nc.dram_tensor("idf", [32, 32], F32, kind="ExternalInput").ap()
    out_hbm = nc.dram_tensor("out", [(NT - 1) * TOK, DM], F32, kind="ExternalOutput").ap()

    sched = Sched()
    ctx = contextlib.ExitStack()
    with ctx:
        sb = lambda name, shape, dt: ctx.enter_context(nc.sbuf_tensor(name, shape, dt))
        pst = lambda name, shape, dt: ctx.enter_context(nc.psum_tensor(name, shape, dt))

        w_sb = {k: sb("w_" + k, [128, CH * DM], BF16) for k in w_hbm}
        x_st = [sb(f"x_st{i}", [128, DM], F32) for i in range(4)]
        x2 = [sb(f"x2_{i}", [128, DM], F32) for i in range(2)]
        x3 = [sb(f"x3_{i}", [128, DM], F32) for i in range(2)]
        h_bf = sb("h_bf", [128, DM], BF16)
        hT = sb("hT", [128, DM], BF16)
        q_bf = sb("q_bf", [128, DM], BF16)
        k_bf = sb("k_bf", [128, DM], BF16)
        qT = [sb(f"qT{i}", [128, DM], BF16) for i in range(2)]
        kT = [sb(f"kT{i}", [128, DM], BF16) for i in range(2)]
        v_sb = [sb(f"v{i}", [128, DM], BF16) for i in range(2)]
        v_halo = [sb(f"vh{i}", [16, DM], BF16) for i in range(2)]
        scr2 = sb("scr2", [128, DM], BF16)
        scr2b = sb("scr2b", [128, DM], BF16)
        A_bf = sb("A_bf", [128, H * 144], BF16)
        ffn_act = sb("ffn_act", [128, DM], BF16)
        xrT_sb = sb("xrT", [128, DM], BF16)
        xkT_sb = sb("xkT", [128, DM], BF16)
        A_T = sb("A_T", [128, DM], BF16)
        A_hT = sb("A_hT", [16, DM], BF16)
        aoT = sb("aoT", [128, DM], BF16)
        h2_bf = sb("h2_bf", [128, DM], BF16)
        h2T = sb("h2T", [128, CH * 132], BF16)
        k_actT = sb("k_actT", [128, DM], BF16)
        dv_sb = sb("dv_sb", [128, DM], BF16)
        sig_r = sb("sig_r", [128, DM], BF16)
        ffn_f = sb("ffn_f", [128, DM], F32)
        smalls = sb("smalls", [128, 96], F32)
        biasc = sb("sb_biasc", [128, H * 128], BF16)
        biash = {k: sb("sb_biash_" + k, [128, H * 16], BF16) for k in ("w", "0", "n")}
        vcols = sb("vcols", [128, 32], F32)
        vecs_st = sb("vecs_st", [32, 128], F32)
        mur_bf = sb("mur_bf", [128, 8], BF16)
        muk_bf = sb("muk_bf", [128, 8], BF16)
        idb = sb("sb_idb", [128, 128], BF16)
        idf = sb("sb_idf", [32, 32], F32)

        ps0 = pst("ps0", [128, 1024], BF16)
        psH = pst("psH", [128, 512], F32)
        ps2 = pst("ps2", [128, 1024], F32)
        ps3 = pst("ps3", [128, 1024], F32)
        ps4 = pst("ps4", [128, 1024], F32)

        B = lambda n: Buf(n)
        b_w = {k: B("w_" + k) for k in w_sb}
        b_xst = [B(f"x_st{i}") for i in range(4)]
        b_x2 = [B(f"x2_{i}") for i in range(2)]
        b_x3 = [B(f"x3_{i}") for i in range(2)]
        b_h = B("h_bf")
        b_hT = B("hT")
        b_qbf = B("q_bf")
        b_kbf = B("k_bf")
        b_qT = [B(f"qT{i}") for i in range(2)]
        b_kT = [B(f"kT{i}") for i in range(2)]
        b_v = [B(f"v{i}") for i in range(2)]
        b_vh = [B(f"vh{i}") for i in range(2)]
        b_scr2 = B("scr2")
        b_scr2b = B("scr2b")
        b_Abf = B("A_bf")
        b_fact = B("ffn_act")
        b_xrT = B("xrT")
        b_xkT = B("xkT")
        b_AT = B("A_T")
        b_AhT = B("A_hT")
        b_aoT = B("aoT")
        b_h2 = B("h2_bf")
        b_h2T = B("h2T")
        b_kactT = B("k_actT")
        b_dv = B("dv_sb")
        b_sigr = B("sig_r")
        b_ffn = B("ffn_f")
        b_biasc = B("biasc")
        b_biash = {k: B("biash_" + k) for k in biash}
        b_vcols = B("vcols")
        b_vecs = B("vecs_st")
        b_mur = B("mur_bf")
        b_muk = B("muk_bf")
        b_idb = B("idb")
        b_idf = B("idf")
        b_ps = {0: B("ps0"), "H": B("psH"), 2: B("ps2"), 3: B("ps3"), 4: B("ps4")}
        b_ln = [B(f"ln{i}") for i in range(4)]
        b_rms = [B("rmsq"), B("rmsk")]
        b_negmax = B("negmax")
        b_negmax2 = B("negmax2")
        b_denom = B("denom")
        b_rden = B("rdenom")
        b_const = B("consts")
        b_sumring = B("sumring")

        G = sched.group

        G("sp", [
            lambda e: e.dma_start(out=vecs_st[:], in_=vecs_hbm),
            lambda e: e.dma_start(out=biasc[:], in_=biasc_hbm),
            lambda e: e.dma_start(out=biash["w"][:], in_=biash_w_hbm),
            lambda e: e.dma_start(out=biash["0"][:], in_=biash_0_hbm),
            lambda e: e.dma_start(out=biash["n"][:], in_=biash_n_hbm),
            lambda e: e.dma_start(out=idb[:], in_=idb_hbm),
            lambda e: e.dma_start(out=idf[:], in_=idf_hbm),
        ], writes=[b_vecs, b_biasc, b_biash["w"], b_biash["0"], b_biash["n"],
                   b_idb, b_idf], dma_chan="wconst", cost=2000)
        for t in (0, 1, 2):
            G("sp", [lambda e, t=t: e.dma_start(
                out=x_st[t % 4][:], in_=x_hbm[t * TOK:(t + 1) * TOK, :])],
              writes=[b_xst[t % 4]], dma_chan=f"x{t % 4}", cost=1600)

        for wname in ("wq", "wk", "wv", "wo", "wr", "wkf", "wvf"):
            src = w_hbm[wname].rearrange("(k p) d -> p k d", p=128)
            dst = w_sb[wname][:].rearrange("p (k d) -> p k d", d=DM)
            G("pool", [lambda e, s=src, d=dst: e.dma_start(out=d, in_=s)],
              writes=[b_w[wname]], dma_chan="w_" + wname, cost=12000)
        G("pool", [lambda e: e.memset(kT[1][:], 0.0)], writes=[b_kT[1]])
        G("pool", [lambda e: e.memset(v_halo[0][:], 0.0)], writes=[b_vh[0]])
        G("pool", [
            lambda e: e.memset(smalls[:, 64:65], 128.0 * EPS_RMS),
            lambda e: e.memset(smalls[:, 65:66], EPS_RMS),
            lambda e: e.memset(smalls[:, 66:67], 1.0),
        ], writes=[b_const])

        G("pe", [lambda e: e.transpose(ps2[:, 0:32], vecs_st[:], idf[:])],
          reads=[b_vecs, b_idf], writes=[b_ps[2]])
        G("dve", [lambda e: e.tensor_copy(vcols[:], ps2[:, 0:32])],
          reads=[b_ps[2]], writes=[b_vcols])
        G("dve", [lambda e: e.tensor_tensor(
            vcols[:, 0:8], vcols[:, 0:8], vcols[:, 8:16], ALU.mult)],
          reads=[b_vcols], writes=[b_vcols], cost=180)
        G("dve", [
            lambda e: e.tensor_copy(mur_bf[:], vcols[:, 16:24]),
            lambda e: e.tensor_copy(muk_bf[:], vcols[:, 24:32]),
        ], reads=[b_vcols], writes=[b_mur, b_muk])

        def ln_smalls(i, sum_ap=None, sum_bufs=()):
            c = 8 * i
            bb = b_ln[i]
            if sum_ap is None:
                sum_ap = smalls[:, c:c + 1]
            G("dve", [lambda e: e.tensor_scalar_mul(
                smalls[:, c + 2:c + 3], sum_ap, -1.0 / DM)],
              reads=[bb] + list(sum_bufs), writes=[bb], cost=160)
            G("dve", [lambda e: e.tensor_scalar(
                smalls[:, c + 3:c + 4], smalls[:, c + 2:c + 3],
                smalls[:, c + 2:c + 3], -1.0, ALU.mult, ALU.mult)],
              reads=[bb], writes=[bb], cost=220)
            G("dve", [lambda e: e.tensor_scalar_add(
                smalls[:, c + 4:c + 5], smalls[:, c + 3:c + 4], EPS_LN)],
              reads=[bb], writes=[bb], cost=160)
            G("act", [lambda e: e.activation(
                smalls[:, c + 5:c + 6], smalls[:, c + 1:c + 2], AF.Ln,
                bias=smalls[:, c + 4:c + 5], scale=1.0 / DM)],
              reads=[bb], writes=[bb], cost=300)
            G("act", [lambda e: e.activation(
                smalls[:, c + 6:c + 7], smalls[:, c + 5:c + 6], AF.Exp,
                scale=-0.5)],
              reads=[bb], writes=[bb], cost=300)

        def ln_cols(i):
            c = 8 * i
            return smalls[:, c + 2:c + 3], smalls[:, c + 6:c + 7]

        def ln_stats(i, src_ap, src_bufs, psum_src=False,
                     sum_out=None, sum_bufs=(), skip_sum=False):
            c = 8 * i
            bb = b_ln[i]
            if not skip_sum:
                s_ap = smalls[:, c:c + 1] if sum_out is None else sum_out
                s_w = [bb] if sum_out is None else list(sum_bufs)
                G("dve", [lambda e, a=s_ap: e.reduce_sum(
                    a, src_ap, axis=AX.X)],
                  reads=src_bufs, writes=s_w, cost=1300)
            G("act", [lambda e: e.activation(scr2[:], src_ap, AF.Square,
                                             accum_out=smalls[:, c + 1:c + 2])],
              reads=src_bufs, writes=[bb], cost=1150)
            if sum_out is None:
                ln_smalls(i)
            else:
                ln_smalls(i, sum_ap=sum_out, sum_bufs=sum_bufs)

        def emit_ln1_stats(t):
            xs, bxs = x_st[t % 4], b_xst[t % 4]
            rc = 88 + t % 4
            ln_stats(0, xs[:], [bxs],
                     sum_out=smalls[:, rc:rc + 1], sum_bufs=[b_sumring])

        def emit_ln1_apply(t):
            xs, bxs = x_st[t % 4], b_xst[t % 4]
            negm1, rs1 = ln_cols(0)
            G("dve", [lambda e: e.tensor_scalar(
                h_bf[:], xs[:], negm1, rs1, ALU.add, ALU.mult)],
              reads=[bxs, b_ln[0]], writes=[b_h], cost=850)

        def transp8(src_sb, src_b, n=CH, cost=1400):
            G("pe", [
                (lambda e, c=c: e.transpose(
                    ps0[:, c * 128:(c + 1) * 128],
                    src_sb[:, c * 128:(c + 1) * 128], idb[:]))
                for c in range(n)
            ], reads=[src_b, b_idb], writes=[b_ps[0]], cost=cost)

        def proj(ps_t, ps_b, lhs_ap_fn, lhs_bufs, wname):
            fns = []
            for c in range(CH):
                for n in range(2):
                    fns.append(lambda e, c=c, n=n: e.matmul(
                        ps_t[:, n * 512:(n + 1) * 512],
                        lhs_ap_fn(c),
                        w_sb[wname][:, c * DM + n * 512: c * DM + n * 512 + 512],
                        start=(c == 0), stop=(c == CH - 1),
                        skip_group_check=True))
            G("pe", fns, reads=list(lhs_bufs) + [b_w[wname]], writes=[ps_b],
              cost=3500)

        def rms(j, ps_t, ps_b, out_sb, out_b, fold_sqrt_d):
            c = 32 + 4 * j
            bb = b_rms[j]
            G("act", [lambda e: e.activation(
                scr2[:], ps_t[:], AF.Square, accum_out=smalls[:, c:c + 1])],
              reads=[ps_b], writes=[bb], cost=1150)
            if fold_sqrt_d:
                sc, bi = 128.0 / DM, smalls[:, 64:65]
            else:
                sc, bi = 1.0 / DM, smalls[:, 65:66]
            G("act", [lambda e: e.activation(
                smalls[:, c + 1:c + 2], smalls[:, c:c + 1], AF.Ln,
                bias=bi, scale=sc)], reads=[bb, b_const], writes=[bb], cost=300)
            G("act", [lambda e: e.activation(
                smalls[:, c + 2:c + 3], smalls[:, c + 1:c + 2], AF.Exp,
                scale=-0.5)], reads=[bb], writes=[bb], cost=300)
            G("act", [lambda e: e.activation(
                out_sb[:], ps_t[:], AF.Identity,
                scale=smalls[:, c + 2:c + 3])],
              reads=[ps_b, bb], writes=[out_b], cost=1250)

        def emit_vhalo(u):
            cu = u % 2
            G("act", [lambda e: e.dma_start(
                out=v_halo[(u + 1) % 2][:], in_=v_sb[cu][112:128, :])],
              reads=[b_v[cu]], writes=[b_vh[(u + 1) % 2]],
              dma_chan=f"vh{(u + 1) % 2}", cost=400)

        def emit_qkT(u):
            cu = u % 2
            transp8(q_bf, b_qbf)
            G("dve", [lambda e: e.tensor_copy(qT[cu][:], ps0[:])],
              reads=[b_ps[0]], writes=[b_qT[cu]], cost=700)
            transp8(k_bf, b_kbf)
            G("dve", [lambda e: e.tensor_tensor(
                kT[cu][:].rearrange("p (c j) -> p c j", c=CH),
                ps0[:].rearrange("p (c j) -> p c j", c=CH),
                vcols[:, 0:8].unsqueeze(2).broadcast_to([128, CH, 128]),
                ALU.mult)],
              reads=[b_ps[0], b_vcols], writes=[b_kT[cu]], cost=1250)

        def emit_ffn_a(u):
            proj(ps4, b_ps[4], lambda c: xrT_sb[:, c * 128:(c + 1) * 128],
                 [b_xrT], "wr")
            G("act", [lambda e: e.activation(
                scr2b[:], ps4[:], AF.Exp, scale=-1.0)],
              reads=[b_ps[4]], writes=[b_scr2b], cost=1150)
            G("act", [lambda e: e.activation(
                ffn_f[:], scr2b[:], AF.Ln, bias=smalls[:, 66:67])],
              reads=[b_scr2b, b_const], writes=[b_ffn], cost=1150)
            G("act", [lambda e: e.activation(
                sig_r[:], ffn_f[:], AF.Exp, scale=-1.0)],
              reads=[b_ffn], writes=[b_sigr], cost=1150)
            proj(ps4, b_ps[4], lambda c: xkT_sb[:, c * 128:(c + 1) * 128],
                 [b_xkT], "wkf")
            G("act", [lambda e: e.activation(scr2b[:], ps4[:], AF.Relu)],
              reads=[b_ps[4]], writes=[b_scr2b], cost=1150)
            G("pool", [lambda e: e.tensor_mul(ffn_act[:], scr2b[:], scr2b[:])],
              reads=[b_scr2b], writes=[b_fact], cost=2100)

        def emit_ffn_b1(u):
            transp8(ffn_act, b_fact)
            G("dve", [lambda e: e.tensor_copy(k_actT[:], ps0[:])],
              reads=[b_ps[0]], writes=[b_kactT], cost=750)

        def emit_ffn_b2(u):
            xs, bxs = x_st[u % 4], b_xst[u % 4]
            proj(ps4, b_ps[4], lambda c: k_actT[:, c * 128:(c + 1) * 128],
                 [b_kactT], "wvf")
            G("dve", [lambda e: e.tensor_tensor(
                ffn_f[:], ps4[:], sig_r[:], ALU.mult)],
              reads=[b_ps[4], b_sigr], writes=[b_ffn], cost=1250)
            ln_stats(3, ffn_f[:], [b_ffn])
            negm4, rs4 = ln_cols(3)
            G("dve", [lambda e: e.tensor_tensor(
                smalls[:, 31:32], negm4, rs4, ALU.mult)],
              reads=[b_ln[3]], writes=[b_ln[3]], cost=180)
            G("act", [lambda e: e.activation(
                xs[:], ffn_f[:], AF.Identity, bias=smalls[:, 31:32], scale=rs4)],
              reads=[b_ffn, b_ln[3]], writes=[bxs], cost=1250)
            G("pool", [lambda e: e.tensor_tensor(
                x3[u % 2][:], x2[u % 2][:], xs[:], ALU.add)],
              reads=[b_x2[u % 2], bxs], writes=[b_x3[u % 2]], cost=2800)
            r = u - 1
            G("sp", [lambda e: e.dma_start(
                out=out_hbm[r * TOK:(r + 1) * TOK, :], in_=x3[u % 2][:])],
              reads=[b_x3[u % 2]], dma_chan=f"out{u % 2}", cost=1600)

        def emit_iter(t):
            warm = (t == 0)
            cur = t % 2
            nxt = t + 1
            xs, bxs = x_st[t % 4], b_xst[t % 4]
            ffn_u = t - 1 if t >= 2 else None

            hsel = biash["w"] if warm else (biash["0"] if t == 1 else biash["n"])
            bsel = b_biash["w"] if warm else (b_biash["0"] if t == 1 else b_biash["n"])
            fns = [
                lambda e: e.matmul(ps2[:, 0:512], idb[:], biasc[:, 0:512],
                                   start=True, stop=False, skip_group_check=True),
                lambda e: e.matmul(ps2[:, 512:1024], idb[:], biasc[:, 512:1024],
                                   start=True, stop=False, skip_group_check=True),
            ]
            for h in range(H):
                fns.append(lambda e, h=h: e.matmul(
                    ps2[:, h * 128:(h + 1) * 128],
                    qT[cur][:, h * 128:(h + 1) * 128],
                    kT[cur][:, h * 128:(h + 1) * 128],
                    start=False, stop=True, skip_group_check=True))
            G("pe", fns, reads=[b_qT[cur], b_kT[cur], b_biasc, b_idb],
              writes=[b_ps[2]], cost=1500)
            fns = [lambda e: e.matmul(
                psH[:, 0:128], idb[:], hsel[:],
                start=True, stop=False, skip_group_check=True)]
            for h in range(H):
                fns.append(lambda e, h=h: e.matmul(
                    psH[:, h * 16:(h + 1) * 16],
                    qT[cur][:, h * 128:(h + 1) * 128],
                    kT[1 - cur][:, h * 128 + 112:h * 128 + 128],
                    start=False, stop=True, skip_group_check=True))
            G("pe", fns, reads=[b_qT[cur], b_kT[1 - cur], bsel, b_idb],
              writes=[b_ps["H"]], cost=800)

            G("dve", [lambda e: e.reduce_max(
                smalls[:, 40:48],
                ps2[:].rearrange("p (h j) -> p h j", h=H), axis=AX.X,
                negate=True)],
              reads=[b_ps[2]], writes=[b_negmax], cost=1400, bias=15000)
            G("dve", [lambda e: e.reduce_max(
                smalls[:, 72:80],
                psH[:, 0:128].rearrange("p (h j) -> p h j", h=H), axis=AX.X,
                negate=True)],
              reads=[b_ps["H"]], writes=[b_negmax2], cost=400, bias=15000)
            G("dve", [lambda e: e.tensor_tensor(
                smalls[:, 40:48], smalls[:, 40:48], smalls[:, 72:80], ALU.min)],
              reads=[b_negmax2], writes=[b_negmax], cost=180, bias=15000)

            if ffn_u is not None:
                emit_ffn_a(ffn_u)

            G("act", [
                (lambda e, h=h: e.activation(
                    A_bf[:, h * 144 + 16:(h + 1) * 144],
                    ps2[:, h * 128:(h + 1) * 128], AF.Exp,
                    bias=smalls[:, 40 + h:41 + h]))
                for h in range(H)
            ], reads=[b_ps[2], b_negmax], writes=[b_Abf], cost=2200, bias=15000)
            G("act", [
                (lambda e, h=h: e.activation(
                    A_bf[:, h * 144:h * 144 + 16],
                    psH[:, h * 16:(h + 1) * 16], AF.Exp,
                    bias=smalls[:, 40 + h:41 + h]))
                for h in range(H)
            ], reads=[b_ps["H"], b_negmax], writes=[b_Abf], cost=1400, bias=15000)
            G("dve", [lambda e: e.reduce_sum(
                smalls[:, 48:56],
                A_bf[:].rearrange("p (h j) -> p h j", h=H), axis=AX.X)],
              reads=[b_Abf], writes=[b_denom], cost=1300, bias=15000)
            G("dve", [lambda e: e.reciprocal(smalls[:, 56:64], smalls[:, 48:56])],
              reads=[b_denom], writes=[b_rden], cost=200, bias=15000)

            if nxt < NT:
                transp8(h_bf, b_h)
                G("dve", [lambda e: e.tensor_copy(hT[:], ps0[:])],
                  reads=[b_ps[0]], writes=[b_hT], cost=700, bias=12000)

            G("dve", [lambda e: e.tensor_tensor(
                A_bf[:].rearrange("p (h j) -> p h j", h=H),
                A_bf[:].rearrange("p (h j) -> p h j", h=H),
                smalls[:, 56:64].unsqueeze(2).broadcast_to([128, H, 144]),
                ALU.mult)],
              reads=[b_rden], writes=[b_Abf], cost=1400, bias=15000)

            G("pe", [
                (lambda e, h=h: e.transpose(
                    ps0[:, h * 128:(h + 1) * 128],
                    A_bf[:, h * 144 + 16:h * 144 + 144], idb[:]))
                for h in range(H)
            ], reads=[b_Abf, b_idb], writes=[b_ps[0]], cost=1400)
            G("dve", [lambda e: e.tensor_copy(A_T[:], ps0[:])],
              reads=[b_ps[0]], writes=[b_AT], cost=700, bias=12000)
            G("pe", [
                (lambda e, h=h: e.transpose(
                    ps0[0:16, h * 128:(h + 1) * 128],
                    A_bf[:, h * 144:h * 144 + 16], idb[:]))
                for h in range(H)
            ], reads=[b_Abf, b_idb], writes=[b_ps[0]], cost=700)
            G("dve", [lambda e: e.tensor_copy(A_hT[:], ps0[0:16, :])],
              reads=[b_ps[0]], writes=[b_AhT], cost=500, bias=12000)

            if nxt < NT:
                proj(ps3, b_ps[3], lambda c: hT[:, c * 128:(c + 1) * 128],
                     [b_hT], "wq")
                rms(0, ps3, b_ps[3], q_bf, b_qbf, True)

            if nxt < NT:
                proj(ps3, b_ps[3], lambda c: hT[:, c * 128:(c + 1) * 128],
                     [b_hT], "wk")
                rms(1, ps3, b_ps[3], k_bf, b_kbf, False)

            fns = []
            for h in range(H):
                fns.append(lambda e, h=h: e.matmul(
                    ps2[:, h * 128:(h + 1) * 128],
                    v_sb[cur][:, h * 128:(h + 1) * 128],
                    A_T[:, h * 128:(h + 1) * 128],
                    start=(h % 4 == 0), stop=False, skip_group_check=True))
            for h in range(H):
                fns.append(lambda e, h=h: e.matmul(
                    ps2[:, h * 128:(h + 1) * 128],
                    v_halo[cur][:, h * 128:(h + 1) * 128],
                    A_hT[:, h * 128:(h + 1) * 128],
                    start=False, stop=True, skip_group_check=True))
            G("pe", fns, reads=[b_v[cur], b_vh[cur], b_AT, b_AhT],
              writes=[b_ps[2]], cost=2400)

            if nxt < NT:
                proj(ps3, b_ps[3], lambda c: hT[:, c * 128:(c + 1) * 128],
                     [b_hT], "wv")
                G("dve", [lambda e: e.tensor_copy(v_sb[nxt % 2][:], ps3[:])],
                  reads=[b_ps[3]], writes=[b_v[nxt % 2]], cost=700)
                emit_vhalo(nxt)

            G("dve", [lambda e: e.tensor_copy(aoT[:], ps2[:])],
              reads=[b_ps[2]], writes=[b_aoT], cost=750, bias=12000)
            proj(ps2, b_ps[2], lambda c: aoT[:, c * 128:(c + 1) * 128],
                 [b_aoT], "wo")

            if nxt < NT:
                emit_qkT(nxt)

            if ffn_u is not None:
                emit_ffn_b1(ffn_u)
                emit_ffn_b2(ffn_u)

            if t + 3 < NT:
                tt_ = t + 3
                G("sp", [lambda e, tt=tt_: e.dma_start(
                    out=x_st[tt % 4][:], in_=x_hbm[tt * TOK:(tt + 1) * TOK, :])],
                  writes=[b_xst[tt_ % 4]], dma_chan=f"x{tt_ % 4}", cost=1600)

            ln_stats(1, ps2[:], [b_ps[2]], psum_src=True)
            negm2, rs2 = ln_cols(1)
            G("dve", [lambda e: e.tensor_tensor(
                smalls[:, 15:16], negm2, rs2, ALU.mult)],
              reads=[b_ln[1]], writes=[b_ln[1]], cost=180)
            G("act", [lambda e: e.activation(
                x3[t % 2][:], ps2[:], AF.Identity, bias=smalls[:, 15:16],
                scale=rs2)],
              reads=[b_ps[2], b_ln[1]], writes=[b_x3[t % 2]], cost=1250)
            G("dve", [lambda e: e.tensor_tensor(
                x2[t % 2][:], xs[:], x3[t % 2][:], ALU.add)],
              reads=[bxs, b_x3[t % 2]], writes=[b_x2[t % 2]], cost=1850)

            rc = 88 + t % 4
            ln_stats(2, x2[t % 2][:], [b_x2[t % 2]], skip_sum=True,
                     sum_out=smalls[:, rc:rc + 1], sum_bufs=[b_sumring])
            negm3, rs3 = ln_cols(2)
            G("dve", [lambda e: e.tensor_scalar(
                h2_bf[:], x2[t % 2][:], negm3, rs3, ALU.add, ALU.mult)],
              reads=[b_x2[t % 2], b_ln[2]], writes=[b_h2], cost=850)

            transp8(h2_bf, b_h2)
            G("dve", [lambda e: e.tensor_copy(
                h2T[:].rearrange("p (c j) -> p c j", c=CH)[:, :, 1:129],
                ps0[:].rearrange("p (c j) -> p c j", c=CH))],
              reads=[b_ps[0]], writes=[b_h2T], cost=750)

            h2c = h2T[:].rearrange("p (c j) -> p c j", c=CH)
            h2cur = h2c[:, :, 1:129]
            h2prev = h2c[:, :, 0:128]
            if not warm:
                dv = dv_sb[:].rearrange("p (c j) -> p c j", c=CH)
                G("dve", [lambda e: e.tensor_tensor(dv, h2cur, h2prev, ALU.subtract)],
                  reads=[b_h2T], writes=[b_dv], cost=700)
                xr3 = xrT_sb[:].rearrange("p (c j) -> p c j", c=CH)
                xk3 = xkT_sb[:].rearrange("p (c j) -> p c j", c=CH)
                h2s = h2_bf[:].rearrange("p (c j) -> p c j", c=CH)
                G("pool", [lambda e: e.tensor_tensor(
                    h2s, dv, mur_bf[:].unsqueeze(2).broadcast_to([128, CH, 128]),
                    ALU.mult)],
                  reads=[b_dv, b_mur], writes=[b_h2], cost=2100)
                G("dve", [lambda e: e.tensor_tensor(xr3, h2s, h2prev, ALU.add)],
                  reads=[b_h2, b_h2T], writes=[b_xrT], cost=2300)
                G("pool", [lambda e: e.tensor_tensor(
                    xk3, dv, muk_bf[:].unsqueeze(2).broadcast_to([128, CH, 128]),
                    ALU.mult)],
                  reads=[b_dv, b_muk], writes=[b_xkT], cost=2100)
                G("pool", [lambda e: e.tensor_tensor(xk3, xk3, h2prev, ALU.add)],
                  reads=[b_xkT, b_h2T], writes=[b_xkT], cost=2100)
            G("pool", [lambda e: e.tensor_copy(h2c[:, :, 0:1], h2c[:, :, 128:129])],
              reads=[b_h2T], writes=[b_h2T], cost=250)

        emit_ln1_stats(0)
        emit_ln1_apply(0)
        transp8(h_bf, b_h)
        G("dve", [lambda e: e.tensor_copy(hT[:], ps0[:])],
          reads=[b_ps[0]], writes=[b_hT], cost=700)
        proj(ps3, b_ps[3], lambda c: hT[:, c * 128:(c + 1) * 128], [b_hT], "wq")
        rms(0, ps3, b_ps[3], q_bf, b_qbf, True)
        proj(ps3, b_ps[3], lambda c: hT[:, c * 128:(c + 1) * 128], [b_hT], "wk")
        rms(1, ps3, b_ps[3], k_bf, b_kbf, False)
        proj(ps3, b_ps[3], lambda c: hT[:, c * 128:(c + 1) * 128], [b_hT], "wv")
        G("dve", [lambda e: e.tensor_copy(v_sb[0][:], ps3[:])],
          reads=[b_ps[3]], writes=[b_v[0]], cost=700)
        emit_vhalo(0)
        emit_qkT(0)
        emit_ln1_stats(1)
        emit_ln1_apply(1)
        emit_ln1_stats(2)

        for t in range(NT):
            emit_iter(t)
            if t + 2 < NT:
                emit_ln1_apply(t + 2)
            if t + 3 < NT:
                emit_ln1_stats(t + 3)
        emit_ffn_a(NT - 1)
        emit_ffn_b1(NT - 1)
        emit_ffn_b2(NT - 1)

        items, dma_count = sched.finalize()
        items["sp"].append(("wait", "dma_out0", dma_count["out0"]))
        items["sp"].append(("wait", "dma_out1", dma_count["out1"]))
        semh = {}
        for e in ENGINES:
            semh[e] = ctx.enter_context(nc.semaphore("sem_" + e))
        for chan in dma_count:
            semh["dma_" + chan] = ctx.enter_context(nc.semaphore("sem_dma_" + chan))
        block = ctx.enter_context(nc.Block())
        SECT = {"pe": block.tensor, "act": block.scalar, "dve": block.vector,
                "pool": block.gpsimd, "sp": block.sync}

        def replay(eng):
            def body(engobj):
                for item in items[eng]:
                    if item[0] == "wait":
                        engobj.wait_ge(semh[item[1]], item[2])
                    elif item[0] == "dma":
                        item[1](engobj).then_inc(semh["dma_" + item[2]], 16)
                    else:
                        inst = item[1](engobj)
                        if item[2]:
                            inst.then_inc(semh[eng], 1)
            return body

        for e in ENGINES:
            SECT[e](replay(e))

    return nc



def _host_consts():
    slopes = np.exp2(-np.arange(1, H + 1, dtype=np.float32))
    i = np.arange(128)[:, None]
    j = np.arange(128)[None, :]
    rel = j - i
    valid = (rel <= 0) & (rel >= -WIN)
    biasc = np.where(valid[None], slopes[:, None, None] * rel[None], NEG)
    biasc = np.transpose(biasc, (1, 0, 2)).reshape(128, H * 128).astype(ml_dtypes.bfloat16)
    jp = np.arange(16)[None, :]
    relh = jp - 16 - i
    validh = relh >= -WIN
    biash_n = np.where(validh[None], slopes[:, None, None] * relh[None], NEG)
    biash_n = np.transpose(biash_n, (1, 0, 2)).reshape(128, H * 16).astype(ml_dtypes.bfloat16)
    biash_w = np.full((128, H * 16), NEG, dtype=ml_dtypes.bfloat16)
    idb = np.eye(128).astype(ml_dtypes.bfloat16)
    idf = np.eye(32).astype(np.float32)
    return biasc, biash_n, biash_w, idb, idf


_CACHE = {}


def kernel(L0_attn_x, L0_attn_wq, L0_attn_wk, L0_attn_wv, L0_attn_wo,
           L0_attn_q_norm_w, L0_attn_k_norm_w,
           L0_ffn_W_r, L0_ffn_W_k, L0_ffn_W_v, L0_ffn_mu_r, L0_ffn_mu_k):
    x = np.ascontiguousarray(np.asarray(L0_attn_x, dtype=np.float32))
    Bsz, S, _ = x.shape
    half = S // 2

    if "nc" not in _CACHE:
        _CACHE["nc"] = build_program()
    nc = _CACHE["nc"]

    biasc, biash_n, biash_w, idb, idf = _host_consts()
    wts = {
        "wq": L0_attn_wq, "wk": L0_attn_wk, "wv": L0_attn_wv, "wo": L0_attn_wo,
        "wr": L0_ffn_W_r, "wkf": L0_ffn_W_k, "wvf": L0_ffn_W_v,
    }
    wT = {k: np.ascontiguousarray(np.asarray(v, dtype=np.float32).T)
          for k, v in wts.items()}
    vecs = np.stack([
        np.asarray(L0_attn_q_norm_w, dtype=np.float32),
        np.asarray(L0_attn_k_norm_w, dtype=np.float32),
        np.asarray(L0_ffn_mu_r, dtype=np.float32),
        np.asarray(L0_ffn_mu_k, dtype=np.float32),
    ]).reshape(32, 128)

    in_maps = []
    for core in range(8):
        b, hf = core // 2, core % 2
        if hf == 0:
            warmrows = np.zeros((TOK, DM), dtype=np.float32)
            bh0 = biash_w
        else:
            warmrows = x[b, half - TOK:half]
            bh0 = biash_n
        x_ext = np.ascontiguousarray(
            np.concatenate([warmrows, x[b, hf * half:(hf + 1) * half]], axis=0))
        m = {"x": x_ext, "vecs": vecs, "biasc": biasc, "biash_w": biash_w,
             "biash_0": bh0, "biash_n": biash_n, "idb": idb, "idf": idf}
        m.update(wT)
        in_maps.append(m)

    res = run_bass_kernel_spmd(nc, in_maps, core_ids=list(range(8)))
    out = np.empty_like(x)
    for core in range(8):
        b, hf = core // 2, core % 2
        out[b, hf * half:(hf + 1) * half] = res.results[core]["out"]
    return out


# version 40
# speedup vs baseline: 1.0266x; 1.0266x over previous
import contextlib

import numpy as np
import ml_dtypes

import concourse.bass as bass
import concourse.mybir as mybir
from concourse.bass_utils import run_bass_kernel_spmd

F32 = mybir.dt.float32
BF16 = mybir.dt.bfloat16
AF = mybir.ActivationFunctionType
ALU = mybir.AluOpType
AX = mybir.AxisListType

TOK = 128
NT = 17
DM = 1024
H = 8
DH = 128
CH = 8
WIN = 16
EPS_LN = 1e-5
EPS_RMS = 1e-6
NEG = -1e30

ENGINES = ("pe", "act", "dve", "pool", "sp")


class Buf:
    __slots__ = ("name", "writer", "readers")

    def __init__(self, name):
        self.name = name
        self.writer = None
        self.readers = []


_DEF_COST = {"pe": 250.0, "act": 1150.0, "dve": 900.0, "pool": 2350.0,
             "sp": 300.0}
_DMA_LAT = 2500.0
_SEM_LAT = 300.0


class Node:
    __slots__ = ("nid", "eng", "fns", "dma_chan", "deps", "cost", "bias",
                 "prio", "start", "finish", "seq")

    def __init__(self, nid, eng, fns, dma_chan, cost, bias):
        self.nid = nid
        self.eng = eng
        self.fns = fns
        self.dma_chan = dma_chan
        self.deps = set()
        self.cost = cost
        self.bias = bias
        self.prio = 0.0
        self.start = 0.0
        self.finish = 0.0
        self.seq = None


class Sched:

    def __init__(self):
        self.nodes = []

    def group(self, eng, fns, reads=(), writes=(), dma_chan=None, cost=None,
              bias=0.0):
        if cost is None:
            cost = _DEF_COST[eng] * len(fns)
        n = Node(len(self.nodes), eng, fns, dma_chan, cost, bias)
        self.nodes.append(n)
        for b in reads:
            if b.writer is not None:
                n.deps.add(b.writer)
        for b in writes:
            if b.writer is not None:
                n.deps.add(b.writer)
            for r in b.readers:
                n.deps.add(r)
        n.deps.discard(n.nid)
        for b in writes:
            b.writer = n.nid
            b.readers = []
        for b in reads:
            b.readers.append(n.nid)
        return n.nid

    def finalize(self):
        nodes = self.nodes
        succs = [[] for _ in nodes]
        npred = [len(n.deps) for n in nodes]
        for n in nodes:
            for d in n.deps:
                succs[d].append(n.nid)
        indeg_out = [len(succs[i]) for i in range(len(nodes))]
        stack = [i for i in range(len(nodes)) if indeg_out[i] == 0]
        topo_r = []
        while stack:
            i = stack.pop()
            topo_r.append(i)
            for d in nodes[i].deps:
                indeg_out[d] -= 1
                if indeg_out[d] == 0:
                    stack.append(d)
        for i in topo_r:
            m = 0.0
            for s in succs[i]:
                if nodes[s].prio > m:
                    m = nodes[s].prio
            nodes[i].prio = nodes[i].cost + nodes[i].bias + m
        WINDOW = 800.0
        eng_time = {e: 0.0 for e in ENGINES}
        ready_time = [0.0] * len(nodes)
        ready = [n.nid for n in nodes if npred[n.nid] == 0]
        sched_order = {e: [] for e in ENGINES}
        scheduled = 0
        while ready:
            s_min = None
            for nid in ready:
                n = nodes[nid]
                s = max(ready_time[nid], eng_time[n.eng])
                if s_min is None or s < s_min:
                    s_min = s
            best = None
            bestp = None
            for nid in ready:
                n = nodes[nid]
                s = max(ready_time[nid], eng_time[n.eng])
                if s <= s_min + WINDOW and (bestp is None or n.prio > bestp):
                    bestp = n.prio
                    best = nid
            nid = best
            ready.remove(nid)
            n = nodes[nid]
            start = max(ready_time[nid], eng_time[n.eng])
            n.start = start
            lat = _DMA_LAT if n.dma_chan is not None else 0.0
            n.finish = start + n.cost + lat
            eng_time[n.eng] = start + n.cost
            sched_order[n.eng].append(nid)
            scheduled += 1
            for s2 in succs[nid]:
                npred[s2] -= 1
                f = n.finish + (0.0 if nodes[s2].eng == n.eng else _SEM_LAT)
                if ready_time[s2] < f:
                    ready_time[s2] = f
                if npred[s2] == 0:
                    ready.append(s2)
        assert scheduled == len(nodes), (scheduled, len(nodes))
        dma_count = {}
        for e in ENGINES:
            seq = 0
            for nid in sched_order[e]:
                n = nodes[nid]
                if n.dma_chan is not None:
                    c = dma_count.setdefault(n.dma_chan, 0)
                    n.seq = ("dma_" + n.dma_chan, c + 16 * len(n.fns))
                    dma_count[n.dma_chan] = c + 16 * len(n.fns)
                else:
                    seq += 1
                    n.seq = (e, seq)
        items = {e: [] for e in ENGINES}
        hwm = {e: {} for e in ENGINES}
        for e in ENGINES:
            for nid in sched_order[e]:
                n = nodes[nid]
                toks = {}
                for d in n.deps:
                    s, v = nodes[d].seq
                    if toks.get(s, 0) < v:
                        toks[s] = v
                for s in sorted(toks):
                    v = toks[s]
                    if hwm[e].get(s, 0) >= v:
                        continue
                    hwm[e][s] = v
                    items[e].append(("wait", s, v))
                if n.dma_chan is not None:
                    for fn in n.fns:
                        items[e].append(("dma", fn, n.dma_chan))
                else:
                    nf = len(n.fns)
                    for i, fn in enumerate(n.fns):
                        items[e].append(("inst", fn, i == nf - 1))
        return items, dma_count


def build_program():
    nc = bass.Bass("TRN2", target_bir_lowering=False)

    x_hbm = nc.dram_tensor("x", [NT * TOK, DM], F32, kind="ExternalInput").ap()
    w_hbm = {}
    for wname in ("wq", "wk", "wv", "wo", "wr", "wkf", "wvf"):
        w_hbm[wname] = nc.dram_tensor(wname, [DM, DM], F32, kind="ExternalInput").ap()
    vecs_hbm = nc.dram_tensor("vecs", [32, 128], F32, kind="ExternalInput").ap()
    biasc_hbm = nc.dram_tensor("biasc", [128, H * 128], BF16, kind="ExternalInput").ap()
    biash_w_hbm = nc.dram_tensor("biash_w", [128, H * 16], BF16, kind="ExternalInput").ap()
    biash_0_hbm = nc.dram_tensor("biash_0", [128, H * 16], BF16, kind="ExternalInput").ap()
    biash_n_hbm = nc.dram_tensor("biash_n", [128, H * 16], BF16, kind="ExternalInput").ap()
    idb_hbm = nc.dram_tensor("idb", [128, 128], BF16, kind="ExternalInput").ap()
    idf_hbm = nc.dram_tensor("idf", [32, 32], F32, kind="ExternalInput").ap()
    out_hbm = nc.dram_tensor("out", [(NT - 1) * TOK, DM], F32, kind="ExternalOutput").ap()

    sched = Sched()
    ctx = contextlib.ExitStack()
    with ctx:
        sb = lambda name, shape, dt: ctx.enter_context(nc.sbuf_tensor(name, shape, dt))
        pst = lambda name, shape, dt: ctx.enter_context(nc.psum_tensor(name, shape, dt))

        w_sb = {k: sb("w_" + k, [128, CH * DM], BF16) for k in w_hbm}
        x_st = [sb(f"x_st{i}", [128, DM], F32) for i in range(4)]
        x2 = [sb(f"x2_{i}", [128, DM], F32) for i in range(2)]
        x3 = [sb(f"x3_{i}", [128, DM], F32) for i in range(2)]
        h_bf = sb("h_bf", [128, DM], BF16)
        hT = sb("hT", [128, DM], BF16)
        q_bf = sb("q_bf", [128, DM], BF16)
        k_bf = sb("k_bf", [128, DM], BF16)
        qT = [sb(f"qT{i}", [128, DM], BF16) for i in range(2)]
        kT = [sb(f"kT{i}", [128, DM], BF16) for i in range(2)]
        v_sb = [sb(f"v{i}", [128, DM], BF16) for i in range(2)]
        v_halo = [sb(f"vh{i}", [16, DM], BF16) for i in range(2)]
        scr2 = sb("scr2", [128, DM], BF16)
        scr2b = sb("scr2b", [128, DM], BF16)
        A_bf = sb("A_bf", [128, H * 144], BF16)
        ffn_act = sb("ffn_act", [128, DM], BF16)
        xrT_sb = sb("xrT", [128, DM], BF16)
        xkT_sb = sb("xkT", [128, DM], BF16)
        A_T = sb("A_T", [128, DM], BF16)
        A_hT = sb("A_hT", [16, DM], BF16)
        aoT = sb("aoT", [128, DM], BF16)
        h2_bf = sb("h2_bf", [128, DM], BF16)
        h2T = sb("h2T", [128, CH * 132], BF16)
        k_actT = sb("k_actT", [128, DM], BF16)
        dv_sb = sb("dv_sb", [128, DM], BF16)
        sig_r = sb("sig_r", [128, DM], BF16)
        ffn_f = sb("ffn_f", [128, DM], F32)
        smalls = sb("smalls", [128, 96], F32)
        biasc = sb("sb_biasc", [128, H * 128], BF16)
        biash = {k: sb("sb_biash_" + k, [128, H * 16], BF16) for k in ("w", "0", "n")}
        vcols = sb("vcols", [128, 32], F32)
        vecs_st = sb("vecs_st", [32, 128], F32)
        mur_bf = sb("mur_bf", [128, 8], BF16)
        muk_bf = sb("muk_bf", [128, 8], BF16)
        idb = sb("sb_idb", [128, 128], BF16)
        idf = sb("sb_idf", [32, 32], F32)

        ps0 = pst("ps0", [128, 1024], BF16)
        psH = pst("psH", [128, 512], F32)
        ps2 = pst("ps2", [128, 1024], F32)
        ps3 = pst("ps3", [128, 1024], F32)
        ps4 = pst("ps4", [128, 1024], F32)

        B = lambda n: Buf(n)
        b_w = {k: B("w_" + k) for k in w_sb}
        b_xst = [B(f"x_st{i}") for i in range(4)]
        b_x2 = [B(f"x2_{i}") for i in range(2)]
        b_x3 = [B(f"x3_{i}") for i in range(2)]
        b_h = B("h_bf")
        b_hT = B("hT")
        b_qbf = B("q_bf")
        b_kbf = B("k_bf")
        b_qT = [B(f"qT{i}") for i in range(2)]
        b_kT = [B(f"kT{i}") for i in range(2)]
        b_v = [B(f"v{i}") for i in range(2)]
        b_vh = [B(f"vh{i}") for i in range(2)]
        b_scr2 = B("scr2")
        b_scr2b = B("scr2b")
        b_Abf = B("A_bf")
        b_fact = B("ffn_act")
        b_xrT = B("xrT")
        b_xkT = B("xkT")
        b_AT = B("A_T")
        b_AhT = B("A_hT")
        b_aoT = B("aoT")
        b_h2 = B("h2_bf")
        b_h2T = B("h2T")
        b_kactT = B("k_actT")
        b_dv = B("dv_sb")
        b_sigr = B("sig_r")
        b_ffn = B("ffn_f")
        b_biasc = B("biasc")
        b_biash = {k: B("biash_" + k) for k in biash}
        b_vcols = B("vcols")
        b_vecs = B("vecs_st")
        b_mur = B("mur_bf")
        b_muk = B("muk_bf")
        b_idb = B("idb")
        b_idf = B("idf")
        b_ps = {0: B("ps0"), "H": B("psH"), 2: B("ps2"), 3: B("ps3"), 4: B("ps4")}
        b_ln = [B(f"ln{i}") for i in range(4)]
        b_rms = [B("rmsq"), B("rmsk")]
        b_negmax = B("negmax")
        b_negmax2 = B("negmax2")
        b_denom = B("denom")
        b_denom2 = B("denom2")
        b_rden = B("rdenom")
        b_const = B("consts")
        b_sumring = B("sumring")

        G = sched.group

        G("sp", [
            lambda e: e.dma_start(out=vecs_st[:], in_=vecs_hbm),
            lambda e: e.dma_start(out=biasc[:], in_=biasc_hbm),
            lambda e: e.dma_start(out=biash["w"][:], in_=biash_w_hbm),
            lambda e: e.dma_start(out=biash["0"][:], in_=biash_0_hbm),
            lambda e: e.dma_start(out=biash["n"][:], in_=biash_n_hbm),
            lambda e: e.dma_start(out=idb[:], in_=idb_hbm),
            lambda e: e.dma_start(out=idf[:], in_=idf_hbm),
        ], writes=[b_vecs, b_biasc, b_biash["w"], b_biash["0"], b_biash["n"],
                   b_idb, b_idf], dma_chan="wconst", cost=2000)
        for t in (0, 1, 2):
            G("sp", [lambda e, t=t: e.dma_start(
                out=x_st[t % 4][:], in_=x_hbm[t * TOK:(t + 1) * TOK, :])],
              writes=[b_xst[t % 4]], dma_chan=f"x{t % 4}", cost=1600)

        for wname in ("wq", "wk", "wv", "wo", "wr", "wkf", "wvf"):
            src = w_hbm[wname].rearrange("(k p) d -> p k d", p=128)
            dst = w_sb[wname][:].rearrange("p (k d) -> p k d", d=DM)
            G("pool", [lambda e, s=src, d=dst: e.dma_start(out=d, in_=s)],
              writes=[b_w[wname]], dma_chan="w_" + wname, cost=12000)
        G("pool", [lambda e: e.memset(kT[1][:], 0.0)], writes=[b_kT[1]])
        G("pool", [lambda e: e.memset(v_halo[0][:], 0.0)], writes=[b_vh[0]])
        G("pool", [
            lambda e: e.memset(smalls[:, 64:65], 128.0 * EPS_RMS),
            lambda e: e.memset(smalls[:, 65:66], EPS_RMS),
            lambda e: e.memset(smalls[:, 66:67], 1.0),
        ], writes=[b_const])

        G("pe", [lambda e: e.transpose(ps2[:, 0:32], vecs_st[:], idf[:])],
          reads=[b_vecs, b_idf], writes=[b_ps[2]])
        G("dve", [lambda e: e.tensor_copy(vcols[:], ps2[:, 0:32])],
          reads=[b_ps[2]], writes=[b_vcols])
        G("dve", [lambda e: e.tensor_tensor(
            vcols[:, 0:8], vcols[:, 0:8], vcols[:, 8:16], ALU.mult)],
          reads=[b_vcols], writes=[b_vcols], cost=180)
        G("dve", [
            lambda e: e.tensor_copy(mur_bf[:], vcols[:, 16:24]),
            lambda e: e.tensor_copy(muk_bf[:], vcols[:, 24:32]),
        ], reads=[b_vcols], writes=[b_mur, b_muk])

        def ln_smalls(i, sum_ap=None, sum_bufs=()):
            c = 8 * i
            bb = b_ln[i]
            if sum_ap is None:
                sum_ap = smalls[:, c:c + 1]
            G("dve", [lambda e: e.tensor_scalar_mul(
                smalls[:, c + 2:c + 3], sum_ap, -1.0 / DM)],
              reads=[bb] + list(sum_bufs), writes=[bb], cost=160)
            G("dve", [lambda e: e.tensor_scalar(
                smalls[:, c + 3:c + 4], smalls[:, c + 2:c + 3],
                smalls[:, c + 2:c + 3], -1.0, ALU.mult, ALU.mult)],
              reads=[bb], writes=[bb], cost=220)
            G("dve", [lambda e: e.tensor_scalar_add(
                smalls[:, c + 4:c + 5], smalls[:, c + 3:c + 4], EPS_LN)],
              reads=[bb], writes=[bb], cost=160)
            G("act", [lambda e: e.activation(
                smalls[:, c + 5:c + 6], smalls[:, c + 1:c + 2], AF.Ln,
                bias=smalls[:, c + 4:c + 5], scale=1.0 / DM)],
              reads=[bb], writes=[bb], cost=300)
            G("act", [lambda e: e.activation(
                smalls[:, c + 6:c + 7], smalls[:, c + 5:c + 6], AF.Exp,
                scale=-0.5)],
              reads=[bb], writes=[bb], cost=300)

        def ln_cols(i):
            c = 8 * i
            return smalls[:, c + 2:c + 3], smalls[:, c + 6:c + 7]

        def ln_stats(i, src_ap, src_bufs, psum_src=False,
                     sum_out=None, sum_bufs=(), skip_sum=False):
            c = 8 * i
            bb = b_ln[i]
            if not skip_sum:
                s_ap = smalls[:, c:c + 1] if sum_out is None else sum_out
                s_w = [bb] if sum_out is None else list(sum_bufs)
                G("dve", [lambda e, a=s_ap: e.reduce_sum(
                    a, src_ap, axis=AX.X)],
                  reads=src_bufs, writes=s_w, cost=1300)
            G("act", [lambda e: e.activation(scr2[:], src_ap, AF.Square,
                                             accum_out=smalls[:, c + 1:c + 2])],
              reads=src_bufs, writes=[bb], cost=1150)
            if sum_out is None:
                ln_smalls(i)
            else:
                ln_smalls(i, sum_ap=sum_out, sum_bufs=sum_bufs)

        def emit_ln1_stats(t):
            xs, bxs = x_st[t % 4], b_xst[t % 4]
            rc = 88 + t % 4
            ln_stats(0, xs[:], [bxs],
                     sum_out=smalls[:, rc:rc + 1], sum_bufs=[b_sumring])

        def emit_ln1_apply(t):
            xs, bxs = x_st[t % 4], b_xst[t % 4]
            negm1, rs1 = ln_cols(0)
            G("dve", [lambda e: e.tensor_scalar(
                h_bf[:], xs[:], negm1, rs1, ALU.add, ALU.mult)],
              reads=[bxs, b_ln[0]], writes=[b_h], cost=850)

        def transp8(src_sb, src_b, n=CH, cost=1400):
            G("pe", [
                (lambda e, c=c: e.transpose(
                    ps0[:, c * 128:(c + 1) * 128],
                    src_sb[:, c * 128:(c + 1) * 128], idb[:]))
                for c in range(n)
            ], reads=[src_b, b_idb], writes=[b_ps[0]], cost=cost)

        def proj(ps_t, ps_b, lhs_ap_fn, lhs_bufs, wname):
            fns = []
            for c in range(CH):
                for n in range(2):
                    fns.append(lambda e, c=c, n=n: e.matmul(
                        ps_t[:, n * 512:(n + 1) * 512],
                        lhs_ap_fn(c),
                        w_sb[wname][:, c * DM + n * 512: c * DM + n * 512 + 512],
                        start=(c == 0), stop=(c == CH - 1),
                        skip_group_check=True))
            G("pe", fns, reads=list(lhs_bufs) + [b_w[wname]], writes=[ps_b],
              cost=3500)

        def rms(j, ps_t, ps_b, out_sb, out_b, fold_sqrt_d):
            c = 32 + 4 * j
            bb = b_rms[j]
            G("act", [lambda e: e.activation(
                scr2[:], ps_t[:], AF.Square, accum_out=smalls[:, c:c + 1])],
              reads=[ps_b], writes=[bb], cost=1150)
            if fold_sqrt_d:
                sc, bi = 128.0 / DM, smalls[:, 64:65]
            else:
                sc, bi = 1.0 / DM, smalls[:, 65:66]
            G("act", [lambda e: e.activation(
                smalls[:, c + 1:c + 2], smalls[:, c:c + 1], AF.Ln,
                bias=bi, scale=sc)], reads=[bb, b_const], writes=[bb], cost=300)
            G("act", [lambda e: e.activation(
                smalls[:, c + 2:c + 3], smalls[:, c + 1:c + 2], AF.Exp,
                scale=-0.5)], reads=[bb], writes=[bb], cost=300)
            G("act", [lambda e: e.activation(
                out_sb[:], ps_t[:], AF.Identity,
                scale=smalls[:, c + 2:c + 3])],
              reads=[ps_b, bb], writes=[out_b], cost=1250)

        def emit_vhalo(u):
            cu = u % 2
            G("act", [lambda e: e.dma_start(
                out=v_halo[(u + 1) % 2][:], in_=v_sb[cu][112:128, :])],
              reads=[b_v[cu]], writes=[b_vh[(u + 1) % 2]],
              dma_chan=f"vh{(u + 1) % 2}", cost=400)

        def emit_qkT(u):
            cu = u % 2
            transp8(q_bf, b_qbf)
            G("dve", [lambda e: e.tensor_copy(qT[cu][:], ps0[:])],
              reads=[b_ps[0]], writes=[b_qT[cu]], cost=700)
            transp8(k_bf, b_kbf)
            G("dve", [lambda e: e.tensor_tensor(
                kT[cu][:].rearrange("p (c j) -> p c j", c=CH),
                ps0[:].rearrange("p (c j) -> p c j", c=CH),
                vcols[:, 0:8].unsqueeze(2).broadcast_to([128, CH, 128]),
                ALU.mult)],
              reads=[b_ps[0], b_vcols], writes=[b_kT[cu]], cost=1250)

        def emit_ffn_a(u):
            proj(ps4, b_ps[4], lambda c: xrT_sb[:, c * 128:(c + 1) * 128],
                 [b_xrT], "wr")
            G("act", [lambda e: e.activation(
                scr2b[:], ps4[:], AF.Exp, scale=-1.0)],
              reads=[b_ps[4]], writes=[b_scr2b], cost=1150)
            G("act", [lambda e: e.activation(
                ffn_f[:], scr2b[:], AF.Ln, bias=smalls[:, 66:67])],
              reads=[b_scr2b, b_const], writes=[b_ffn], cost=1150)
            G("act", [lambda e: e.activation(
                sig_r[:], ffn_f[:], AF.Exp, scale=-1.0)],
              reads=[b_ffn], writes=[b_sigr], cost=1150)
            proj(ps4, b_ps[4], lambda c: xkT_sb[:, c * 128:(c + 1) * 128],
                 [b_xkT], "wkf")
            G("act", [lambda e: e.activation(scr2b[:], ps4[:], AF.Relu)],
              reads=[b_ps[4]], writes=[b_scr2b], cost=1150)
            G("pool", [lambda e: e.tensor_mul(ffn_act[:], scr2b[:], scr2b[:])],
              reads=[b_scr2b], writes=[b_fact], cost=2100)

        def emit_ffn_b1(u):
            transp8(ffn_act, b_fact)
            G("dve", [lambda e: e.tensor_copy(k_actT[:], ps0[:])],
              reads=[b_ps[0]], writes=[b_kactT], cost=750)

        def emit_ffn_b2(u):
            xs, bxs = x_st[u % 4], b_xst[u % 4]
            proj(ps4, b_ps[4], lambda c: k_actT[:, c * 128:(c + 1) * 128],
                 [b_kactT], "wvf")
            G("dve", [lambda e: e.tensor_tensor(
                ffn_f[:], ps4[:], sig_r[:], ALU.mult)],
              reads=[b_ps[4], b_sigr], writes=[b_ffn], cost=1250)
            ln_stats(3, ffn_f[:], [b_ffn])
            negm4, rs4 = ln_cols(3)
            G("dve", [lambda e: e.tensor_scalar(
                xs[:], ffn_f[:], negm4, rs4, ALU.add, ALU.mult)],
              reads=[b_ffn, b_ln[3]], writes=[bxs], cost=1250)
            G("pool", [lambda e: e.tensor_tensor(
                x3[u % 2][:], x2[u % 2][:], xs[:], ALU.add)],
              reads=[b_x2[u % 2], bxs], writes=[b_x3[u % 2]], cost=2800)
            r = u - 1
            G("sp", [lambda e: e.dma_start(
                out=out_hbm[r * TOK:(r + 1) * TOK, :], in_=x3[u % 2][:])],
              reads=[b_x3[u % 2]], dma_chan=f"out{u % 2}", cost=1600)

        def emit_iter(t):
            warm = (t == 0)
            cur = t % 2
            nxt = t + 1
            xs, bxs = x_st[t % 4], b_xst[t % 4]
            ffn_u = t - 1 if t >= 2 else None

            hsel = biash["w"] if warm else (biash["0"] if t == 1 else biash["n"])
            bsel = b_biash["w"] if warm else (b_biash["0"] if t == 1 else b_biash["n"])
            fns = [
                lambda e: e.matmul(ps2[:, 0:512], idb[:], biasc[:, 0:512],
                                   start=True, stop=False, skip_group_check=True),
                lambda e: e.matmul(ps2[:, 512:1024], idb[:], biasc[:, 512:1024],
                                   start=True, stop=False, skip_group_check=True),
            ]
            for h in range(H):
                fns.append(lambda e, h=h: e.matmul(
                    ps2[:, h * 128:(h + 1) * 128],
                    qT[cur][:, h * 128:(h + 1) * 128],
                    kT[cur][:, h * 128:(h + 1) * 128],
                    start=False, stop=True, skip_group_check=True))
            G("pe", fns, reads=[b_qT[cur], b_kT[cur], b_biasc, b_idb],
              writes=[b_ps[2]], cost=1500)
            fns = [lambda e: e.matmul(
                psH[:, 0:128], idb[:], hsel[:],
                start=True, stop=False, skip_group_check=True)]
            for h in range(H):
                fns.append(lambda e, h=h: e.matmul(
                    psH[:, h * 16:(h + 1) * 16],
                    qT[cur][:, h * 128:(h + 1) * 128],
                    kT[1 - cur][:, h * 128 + 112:h * 128 + 128],
                    start=False, stop=True, skip_group_check=True))
            G("pe", fns, reads=[b_qT[cur], b_kT[1 - cur], bsel, b_idb],
              writes=[b_ps["H"]], cost=800)

            G("dve", [lambda e: e.reduce_max(
                smalls[:, 40:48],
                ps2[:].rearrange("p (h j) -> p h j", h=H), axis=AX.X,
                negate=True)],
              reads=[b_ps[2]], writes=[b_negmax], cost=1400, bias=15000)
            G("dve", [lambda e: e.reduce_max(
                smalls[:, 72:80],
                psH[:, 0:128].rearrange("p (h j) -> p h j", h=H), axis=AX.X,
                negate=True)],
              reads=[b_ps["H"]], writes=[b_negmax2], cost=400, bias=15000)
            G("dve", [lambda e: e.tensor_tensor(
                smalls[:, 40:48], smalls[:, 40:48], smalls[:, 72:80], ALU.min)],
              reads=[b_negmax2], writes=[b_negmax], cost=180, bias=15000)

            if ffn_u is not None:
                emit_ffn_a(ffn_u)

            G("act", [
                (lambda e, h=h: e.activation(
                    A_bf[:, h * 144 + 16:(h + 1) * 144],
                    ps2[:, h * 128:(h + 1) * 128], AF.Exp,
                    bias=smalls[:, 40 + h:41 + h],
                    accum_out=smalls[:, 48 + h:49 + h]))
                for h in range(H)
            ], reads=[b_ps[2], b_negmax], writes=[b_Abf, b_denom],
              cost=2200, bias=15000)
            G("act", [
                (lambda e, h=h: e.activation(
                    A_bf[:, h * 144:h * 144 + 16],
                    psH[:, h * 16:(h + 1) * 16], AF.Exp,
                    bias=smalls[:, 40 + h:41 + h],
                    accum_out=smalls[:, 80 + h:81 + h]))
                for h in range(H)
            ], reads=[b_ps["H"], b_negmax], writes=[b_Abf, b_denom2],
              cost=1400, bias=15000)
            G("dve", [lambda e: e.tensor_tensor(
                smalls[:, 48:56], smalls[:, 48:56], smalls[:, 80:88], ALU.add)],
              reads=[b_denom2], writes=[b_denom], cost=160, bias=15000)
            G("dve", [lambda e: e.reciprocal(smalls[:, 56:64], smalls[:, 48:56])],
              reads=[b_denom], writes=[b_rden], cost=200, bias=15000)

            if nxt < NT:
                transp8(h_bf, b_h)
                G("dve", [lambda e: e.tensor_copy(hT[:], ps0[:])],
                  reads=[b_ps[0]], writes=[b_hT], cost=700, bias=12000)

            G("dve", [lambda e: e.tensor_tensor(
                A_bf[:].rearrange("p (h j) -> p h j", h=H),
                A_bf[:].rearrange("p (h j) -> p h j", h=H),
                smalls[:, 56:64].unsqueeze(2).broadcast_to([128, H, 144]),
                ALU.mult)],
              reads=[b_rden], writes=[b_Abf], cost=1400, bias=15000)

            G("pe", [
                (lambda e, h=h: e.transpose(
                    ps0[:, h * 128:(h + 1) * 128],
                    A_bf[:, h * 144 + 16:h * 144 + 144], idb[:]))
                for h in range(H)
            ], reads=[b_Abf, b_idb], writes=[b_ps[0]], cost=1400)
            G("dve", [lambda e: e.tensor_copy(A_T[:], ps0[:])],
              reads=[b_ps[0]], writes=[b_AT], cost=700, bias=12000)
            G("pe", [
                (lambda e, h=h: e.transpose(
                    ps0[0:16, h * 128:(h + 1) * 128],
                    A_bf[:, h * 144:h * 144 + 16], idb[:]))
                for h in range(H)
            ], reads=[b_Abf, b_idb], writes=[b_ps[0]], cost=700)
            G("dve", [lambda e: e.tensor_copy(A_hT[:], ps0[0:16, :])],
              reads=[b_ps[0]], writes=[b_AhT], cost=500, bias=12000)

            if nxt < NT:
                proj(ps3, b_ps[3], lambda c: hT[:, c * 128:(c + 1) * 128],
                     [b_hT], "wq")
                rms(0, ps3, b_ps[3], q_bf, b_qbf, True)

            if nxt < NT:
                proj(ps3, b_ps[3], lambda c: hT[:, c * 128:(c + 1) * 128],
                     [b_hT], "wk")
                rms(1, ps3, b_ps[3], k_bf, b_kbf, False)

            fns = []
            for h in range(H):
                fns.append(lambda e, h=h: e.matmul(
                    ps2[:, h * 128:(h + 1) * 128],
                    v_sb[cur][:, h * 128:(h + 1) * 128],
                    A_T[:, h * 128:(h + 1) * 128],
                    start=(h % 4 == 0), stop=False, skip_group_check=True))
            for h in range(H):
                fns.append(lambda e, h=h: e.matmul(
                    ps2[:, h * 128:(h + 1) * 128],
                    v_halo[cur][:, h * 128:(h + 1) * 128],
                    A_hT[:, h * 128:(h + 1) * 128],
                    start=False, stop=True, skip_group_check=True))
            G("pe", fns, reads=[b_v[cur], b_vh[cur], b_AT, b_AhT],
              writes=[b_ps[2]], cost=2400)

            if nxt < NT:
                proj(ps3, b_ps[3], lambda c: hT[:, c * 128:(c + 1) * 128],
                     [b_hT], "wv")
                G("dve", [lambda e: e.tensor_copy(v_sb[nxt % 2][:], ps3[:])],
                  reads=[b_ps[3]], writes=[b_v[nxt % 2]], cost=700)
                emit_vhalo(nxt)

            G("dve", [lambda e: e.tensor_copy(aoT[:], ps2[:])],
              reads=[b_ps[2]], writes=[b_aoT], cost=750, bias=12000)
            proj(ps2, b_ps[2], lambda c: aoT[:, c * 128:(c + 1) * 128],
                 [b_aoT], "wo")

            if nxt < NT:
                emit_qkT(nxt)

            if ffn_u is not None:
                emit_ffn_b1(ffn_u)
                emit_ffn_b2(ffn_u)

            if t + 3 < NT:
                tt_ = t + 3
                G("sp", [lambda e, tt=tt_: e.dma_start(
                    out=x_st[tt % 4][:], in_=x_hbm[tt * TOK:(tt + 1) * TOK, :])],
                  writes=[b_xst[tt_ % 4]], dma_chan=f"x{tt_ % 4}", cost=1600)

            ln_stats(1, ps2[:], [b_ps[2]], psum_src=True)
            negm2, rs2 = ln_cols(1)
            G("dve", [lambda e: e.tensor_scalar(
                x3[t % 2][:], ps2[:], negm2, rs2, ALU.add, ALU.mult)],
              reads=[b_ps[2], b_ln[1]], writes=[b_x3[t % 2]], cost=1250,
              bias=6000)
            G("pool", [lambda e: e.tensor_tensor(
                x2[t % 2][:], xs[:], x3[t % 2][:], ALU.add)],
              reads=[bxs, b_x3[t % 2]], writes=[b_x2[t % 2]], cost=2100,
              bias=6000)

            rc = 88 + t % 4
            ln_stats(2, x2[t % 2][:], [b_x2[t % 2]], skip_sum=True,
                     sum_out=smalls[:, rc:rc + 1], sum_bufs=[b_sumring])
            negm3, rs3 = ln_cols(2)
            G("dve", [lambda e: e.tensor_scalar(
                h2_bf[:], x2[t % 2][:], negm3, rs3, ALU.add, ALU.mult)],
              reads=[b_x2[t % 2], b_ln[2]], writes=[b_h2], cost=850)

            transp8(h2_bf, b_h2)
            G("dve", [lambda e: e.tensor_copy(
                h2T[:].rearrange("p (c j) -> p c j", c=CH)[:, :, 1:129],
                ps0[:].rearrange("p (c j) -> p c j", c=CH))],
              reads=[b_ps[0]], writes=[b_h2T], cost=750)

            h2c = h2T[:].rearrange("p (c j) -> p c j", c=CH)
            h2cur = h2c[:, :, 1:129]
            h2prev = h2c[:, :, 0:128]
            if not warm:
                dv = dv_sb[:].rearrange("p (c j) -> p c j", c=CH)
                G("dve", [lambda e: e.tensor_tensor(dv, h2cur, h2prev, ALU.subtract)],
                  reads=[b_h2T], writes=[b_dv], cost=700)
                xr3 = xrT_sb[:].rearrange("p (c j) -> p c j", c=CH)
                xk3 = xkT_sb[:].rearrange("p (c j) -> p c j", c=CH)
                h2s = h2_bf[:].rearrange("p (c j) -> p c j", c=CH)
                G("pool", [lambda e: e.tensor_tensor(
                    h2s, dv, mur_bf[:].unsqueeze(2).broadcast_to([128, CH, 128]),
                    ALU.mult)],
                  reads=[b_dv, b_mur], writes=[b_h2], cost=2100)
                G("dve", [lambda e: e.tensor_tensor(xr3, h2s, h2prev, ALU.add)],
                  reads=[b_h2, b_h2T], writes=[b_xrT], cost=2300)
                G("pool", [lambda e: e.tensor_tensor(
                    xk3, dv, muk_bf[:].unsqueeze(2).broadcast_to([128, CH, 128]),
                    ALU.mult)],
                  reads=[b_dv, b_muk], writes=[b_xkT], cost=2100)
                G("pool", [lambda e: e.tensor_tensor(xk3, xk3, h2prev, ALU.add)],
                  reads=[b_xkT, b_h2T], writes=[b_xkT], cost=2100)
            G("pool", [lambda e: e.tensor_copy(h2c[:, :, 0:1], h2c[:, :, 128:129])],
              reads=[b_h2T], writes=[b_h2T], cost=250)

        emit_ln1_stats(0)
        emit_ln1_apply(0)
        transp8(h_bf, b_h)
        G("dve", [lambda e: e.tensor_copy(hT[:], ps0[:])],
          reads=[b_ps[0]], writes=[b_hT], cost=700)
        proj(ps3, b_ps[3], lambda c: hT[:, c * 128:(c + 1) * 128], [b_hT], "wq")
        rms(0, ps3, b_ps[3], q_bf, b_qbf, True)
        proj(ps3, b_ps[3], lambda c: hT[:, c * 128:(c + 1) * 128], [b_hT], "wk")
        rms(1, ps3, b_ps[3], k_bf, b_kbf, False)
        proj(ps3, b_ps[3], lambda c: hT[:, c * 128:(c + 1) * 128], [b_hT], "wv")
        G("dve", [lambda e: e.tensor_copy(v_sb[0][:], ps3[:])],
          reads=[b_ps[3]], writes=[b_v[0]], cost=700)
        emit_vhalo(0)
        emit_qkT(0)
        emit_ln1_stats(1)
        emit_ln1_apply(1)
        emit_ln1_stats(2)

        for t in range(NT):
            emit_iter(t)
            if t + 2 < NT:
                emit_ln1_apply(t + 2)
            if t + 3 < NT:
                emit_ln1_stats(t + 3)
        emit_ffn_a(NT - 1)
        emit_ffn_b1(NT - 1)
        emit_ffn_b2(NT - 1)

        items, dma_count = sched.finalize()
        items["sp"].append(("wait", "dma_out0", dma_count["out0"]))
        items["sp"].append(("wait", "dma_out1", dma_count["out1"]))
        semh = {}
        for e in ENGINES:
            semh[e] = ctx.enter_context(nc.semaphore("sem_" + e))
        for chan in dma_count:
            semh["dma_" + chan] = ctx.enter_context(nc.semaphore("sem_dma_" + chan))
        block = ctx.enter_context(nc.Block())
        SECT = {"pe": block.tensor, "act": block.scalar, "dve": block.vector,
                "pool": block.gpsimd, "sp": block.sync}

        def replay(eng):
            def body(engobj):
                for item in items[eng]:
                    if item[0] == "wait":
                        engobj.wait_ge(semh[item[1]], item[2])
                    elif item[0] == "dma":
                        item[1](engobj).then_inc(semh["dma_" + item[2]], 16)
                    else:
                        inst = item[1](engobj)
                        if item[2]:
                            inst.then_inc(semh[eng], 1)
            return body

        for e in ENGINES:
            SECT[e](replay(e))

    return nc



def _host_consts():
    slopes = np.exp2(-np.arange(1, H + 1, dtype=np.float32))
    i = np.arange(128)[:, None]
    j = np.arange(128)[None, :]
    rel = j - i
    valid = (rel <= 0) & (rel >= -WIN)
    biasc = np.where(valid[None], slopes[:, None, None] * rel[None], NEG)
    biasc = np.transpose(biasc, (1, 0, 2)).reshape(128, H * 128).astype(ml_dtypes.bfloat16)
    jp = np.arange(16)[None, :]
    relh = jp - 16 - i
    validh = relh >= -WIN
    biash_n = np.where(validh[None], slopes[:, None, None] * relh[None], NEG)
    biash_n = np.transpose(biash_n, (1, 0, 2)).reshape(128, H * 16).astype(ml_dtypes.bfloat16)
    biash_w = np.full((128, H * 16), NEG, dtype=ml_dtypes.bfloat16)
    idb = np.eye(128).astype(ml_dtypes.bfloat16)
    idf = np.eye(32).astype(np.float32)
    return biasc, biash_n, biash_w, idb, idf


_CACHE = {}


def kernel(L0_attn_x, L0_attn_wq, L0_attn_wk, L0_attn_wv, L0_attn_wo,
           L0_attn_q_norm_w, L0_attn_k_norm_w,
           L0_ffn_W_r, L0_ffn_W_k, L0_ffn_W_v, L0_ffn_mu_r, L0_ffn_mu_k):
    x = np.ascontiguousarray(np.asarray(L0_attn_x, dtype=np.float32))
    Bsz, S, _ = x.shape
    half = S // 2

    if "nc" not in _CACHE:
        _CACHE["nc"] = build_program()
    nc = _CACHE["nc"]

    biasc, biash_n, biash_w, idb, idf = _host_consts()
    wts = {
        "wq": L0_attn_wq, "wk": L0_attn_wk, "wv": L0_attn_wv, "wo": L0_attn_wo,
        "wr": L0_ffn_W_r, "wkf": L0_ffn_W_k, "wvf": L0_ffn_W_v,
    }
    wT = {k: np.ascontiguousarray(np.asarray(v, dtype=np.float32).T)
          for k, v in wts.items()}
    vecs = np.stack([
        np.asarray(L0_attn_q_norm_w, dtype=np.float32),
        np.asarray(L0_attn_k_norm_w, dtype=np.float32),
        np.asarray(L0_ffn_mu_r, dtype=np.float32),
        np.asarray(L0_ffn_mu_k, dtype=np.float32),
    ]).reshape(32, 128)

    in_maps = []
    for core in range(8):
        b, hf = core // 2, core % 2
        if hf == 0:
            warmrows = np.zeros((TOK, DM), dtype=np.float32)
            bh0 = biash_w
        else:
            warmrows = x[b, half - TOK:half]
            bh0 = biash_n
        x_ext = np.ascontiguousarray(
            np.concatenate([warmrows, x[b, hf * half:(hf + 1) * half]], axis=0))
        m = {"x": x_ext, "vecs": vecs, "biasc": biasc, "biash_w": biash_w,
             "biash_0": bh0, "biash_n": biash_n, "idb": idb, "idf": idf}
        m.update(wT)
        in_maps.append(m)

    res = run_bass_kernel_spmd(nc, in_maps, core_ids=list(range(8)))
    out = np.empty_like(x)
    for core in range(8):
        b, hf = core // 2, core % 2
        out[b, hf * half:(hf + 1) * half] = res.results[core]["out"]
    return out
